# version 15
# baseline (speedup 1.0000x reference)
import math
import numpy as np
from contextlib import ExitStack

import concourse.bass as bass
import concourse.mybir as mybir
from concourse.bass_utils import run_bass_kernel_spmd

F32 = mybir.dt.float32
BF16 = mybir.dt.bfloat16
AF = mybir.ActivationFunctionType
OP = mybir.AluOpType
AX = mybir.AxisListType

T = 2048
D = 2048
NH = 16
DH = 128
FR = 1408
NE = 8
FS = 2816
NT = T // 128
ND = D // 128
NFR = FR // 128
NFS = FS // 128
SOFT_SHIFT = 25.0
SCALE_QK = 1.0 / math.sqrt(128.0)
EPS = 1e-5
CONST_VALS = [0.0, 1.0, EPS, -SOFT_SHIFT, SCALE_QK]

ENGINES = ("sync", "tensor", "vector", "scalar", "gpsimd")

ARN_COLS = 66560
XLNT0 = 0


class Prog:

    def __init__(self):
        self.ops = {e: [] for e in ENGINES}
        self.cnt = {}

    def emit(self, eng, method, *args, waits=(), inc=None, n=1, **kwargs):
        tok = None
        if inc is not None:
            self.cnt[inc] = self.cnt.get(inc, 0) + n
            tok = (inc, self.cnt[inc])
        ws = []
        for w in waits:
            if w is None:
                continue
            if isinstance(w, list):
                ws.extend(x for x in w if x is not None)
            else:
                ws.append(w)
        self.ops[eng].append((method, args, kwargs, ws, inc, n))
        return tok

    def dma(self, eng, out, in_, waits=(), inc=None, **kw):
        return self.emit(eng, "dma_start", out, in_, waits=waits, inc=inc, n=16, **kw)

    def dmat(self, out, in_, waits=(), inc="TR", eng="sync"):
        return self.emit(eng, "dma_start_transpose", out, in_, waits=waits, inc=inc, n=16)

    def full(self, sem):
        c = self.cnt.get(sem, 0)
        return (sem, c) if c > 0 else None

    def realize(self, nc):
        semnames = sorted(self.cnt.keys())
        with ExitStack() as st:
            sems = {s: st.enter_context(nc.semaphore(s)) for s in semnames}
            block = st.enter_context(nc.Block())
            for eng_name in ENGINES:
                ops = self.ops[eng_name]
                if not ops:
                    continue

                def make(ops=ops):
                    def body(eng):
                        waited = {}
                        for method, args, kwargs, ws, inc, n in ops:
                            for s, v in ws:
                                if waited.get(s, 0) < v:
                                    eng.wait_ge(sems[s], v)
                                    waited[s] = v
                            ins = getattr(eng, method)(*args, **kwargs)
                            if inc is not None:
                                ins.then_inc(sems[inc], n)
                    return body

                getattr(block, eng_name)(make())


class K:
    pass



def tile6(wT, nblk, blk_w=128):
    R = wT.shape[0]
    nr = R // 128
    a = wT.reshape(nr, 128, nblk, blk_w).transpose(2, 1, 0, 3)
    return np.ascontiguousarray(a.reshape(nblk, 128, nr * blk_w))


def host_prep(inputs):
    g = {}
    c = lambda a: np.ascontiguousarray(np.asarray(a, dtype=np.float32))
    g["l0_wq6"] = tile6(c(np.transpose(inputs["L0_attn_W_Q"], (2, 0, 1)).reshape(D, D)), NH)
    g["l0_wk6"] = tile6(c(np.transpose(inputs["L0_attn_W_K"], (2, 0, 1)).reshape(D, D)), NH)
    g["l0_wv6"] = tile6(c(np.transpose(inputs["L0_attn_W_V"], (2, 0, 1)).reshape(D, D)), NH)
    g["l0_woT"] = c(inputs["L0_attn_W_O"].T)
    g["l1_wq6"] = tile6(c(inputs["L1_attn_Wq"].T), NH)
    g["l1_wk6"] = tile6(c(inputs["L1_attn_Wk"].T), NH)
    g["l1_wv6"] = tile6(c(inputs["L1_attn_Wv"].T), NH)
    g["l1_woT"] = c(inputs["L1_attn_Wo"].T)
    gwT = c(inputs["L0_ffn_gate_weight"].T)
    g["gw6"] = np.ascontiguousarray(
        gwT.reshape(ND, 128, NE).transpose(1, 0, 2).reshape(128, ND * NE))
    wgT = c(np.transpose(inputs["L0_ffn_w_gate"], (0, 2, 1)))
    wuT = c(np.transpose(inputs["L0_ffn_w_up"], (0, 2, 1)))
    g["wg6"] = np.ascontiguousarray(np.stack([tile6(wgT[e], NFR) for e in range(NE)]))
    g["wu6"] = np.ascontiguousarray(np.stack([tile6(wuT[e], NFR) for e in range(NE)]))
    g["wdT"] = c(np.transpose(inputs["L0_ffn_w_down"], (0, 2, 1)))
    g["swg6"] = tile6(c(inputs["L0_ffn_sw_gate"].T), NFS)
    g["swu6"] = tile6(c(inputs["L0_ffn_sw_up"].T), NFS)
    g["swdT"] = c(inputs["L0_ffn_sw_down"].T)
    g["factors"] = c(inputs["L1_ffn_factors"])
    g["bias_t"] = c(np.tile(np.asarray(inputs["L1_ffn_bias"], np.float32)[None, :], (128, 1)))
    inv = 1.0 / (10000.0 ** (np.arange(0, DH, 2, dtype=np.float64) / DH))
    fr = np.outer(inv, np.arange(T, dtype=np.float64))
    cos = np.cos(fr)
    sin = np.sin(fr)
    g["cos2"] = c(np.concatenate([cos, cos], axis=0))
    g["sin2s"] = c(np.concatenate([-sin, sin], axis=0))
    perm = np.zeros((128, 128), dtype=np.float32)
    for h in range(128):
        perm[(h + 64) % 128, h] = 1.0
    g["perm"] = c(perm)
    p = np.arange(128)[:, None]
    q = np.arange(512)[None, :]
    masks = np.stack([(p <= q - 128 * k).astype(np.float32) for k in range(4)])
    g["masks"] = c(masks.reshape(4 * 128, 512))
    g["consts"] = c(np.tile(np.array(CONST_VALS, dtype=np.float64)[None, :], (128, 1)))
    return g


WSPECS = [
    ("l0_wq6", (NH, 128, D)), ("l0_wk6", (NH, 128, D)), ("l0_wv6", (NH, 128, D)),
    ("l0_woT", (D, D)),
    ("l1_wq6", (NH, 128, D)), ("l1_wk6", (NH, 128, D)), ("l1_wv6", (NH, 128, D)),
    ("l1_woT", (D, D)),
    ("gw6", (128, ND * NE)),
    ("wg6", (NE, NFR, 128, D)), ("wu6", (NE, NFR, 128, D)), ("wdT", (NE, FR, D)),
    ("swg6", (NFS, 128, D)), ("swu6", (NFS, 128, D)), ("swdT", (FS, D)),
    ("factors", (NE, D, D)), ("bias_t", (128, D)),
    ("cos2", (128, T)), ("sin2s", (128, T)), ("perm", (128, 128)),
    ("masks", (4 * 128, 512)), ("consts", (128, len(CONST_VALS))),
]



def colsrc(w, col0, width, p=128):
    return w[:, col0:col0 + width].rearrange("(c p) h -> c p h", p=p).transpose([1, 0, 2])


def coldst(sb_ap, width):
    return sb_ap.rearrange("p (c h) -> p c h", h=width)


def setup(k, st, nc, wnames):
    k.nc = nc
    p = k.p
    k.w = {}
    for name, shape in WSPECS:
        if name in wnames:
            k.w[name] = nc.dram_tensor(name, list(shape), F32, kind="ExternalInput").ap()

    k.xln_s = nc.dram_tensor("xln_s", [T, D], BF16, kind="Internal").ap()
    k.ctx2 = nc.dram_tensor("ctx2", [NT, 128, NH, 128], BF16, kind="Internal").ap()
    k.g_s = [nc.dram_tensor("g_s0", [NT, 128, NFR, 128], BF16, kind="Internal").ap(),
             nc.dram_tensor("g_s1", [NT, 128, NFR, 128], BF16, kind="Internal").ap()]
    k.fsum_s = nc.dram_tensor("fsum_s", [D, D], BF16, kind="Internal").ap()

    sb = lambda name, shape, dt: st.enter_context(nc.sbuf_tensor(name, shape, dt))
    k.ARN = sb("ARN", [128, ARN_COLS], BF16)
    k.S = sb("S", [128, 4096], F32)
    k.qt = sb("qt", [128, T], BF16)
    k.kt = sb("kt", [128, T], BF16)
    k.eb = sb("eb", [128, 2048], BF16)
    k.masks = sb("masks_sb", [128, 2048], BF16)
    k.vst = sb("vst", [128, 2048], BF16)
    k.qtb = sb("qtb", [128, 1024], BF16)
    k.xlt = sb("xlt", [128, 2 * D], BF16)
    k.cos2 = sb("cos2_sb", [128, T], BF16)
    k.sin2s = sb("sin2s_sb", [128, T], BF16)
    k.perm = sb("perm_sb", [128, 128], BF16)
    k.gw = sb("gw_sb", [128, ND * NE], BF16)
    k.ones = sb("ones_sb", [128, 128], BF16)
    k.ob = sb("ob", [128, 1024], F32)
    k.sg = sb("sg", [128, 1024], F32)
    k.rtmp = sb("rtmp", [128, 1024], F32)
    k.rd = sb("rd", [128, 512], F32)
    k.acc = sb("acc", [128, 512], F32)
    k.cw = sb("cw_sb", [128, NT * NE], F32)
    k.stats = sb("stats_sb", [128, 24], F32)
    k.mv = sb("mv_sb", [128, 2], F32)
    k.rstd = sb("rstd_sb", [128, 2], F32)
    k.rt = sb("rt_sb", [128, 24], F32)
    k.consts = sb("consts_sb", [128, len(CONST_VALS)], F32)
    for i, cv in enumerate(CONST_VALS):
        nc.const_aps.register(F32, float(cv), k.consts[:, i:i + 1])
        if float(np.float32(cv)) != float(cv):
            nc.const_aps.register(F32, float(np.float32(cv)), k.consts[:, i:i + 1])
    k.P = [st.enter_context(nc.psum_tensor(f"ps{i}", [128, 512], F32)) for i in range(8)]
    k.pwar = [[] for _ in range(8)]
    k.sbuf_left = nc.sbuf_bytes_remaining

    k.tok_cos = p.dma("gpsimd", k.cos2[:], k.w["cos2"][:, :], inc="IC") \
        if "cos2" in k.w else None
    k.tok_sin = p.dma("gpsimd", k.sin2s[:], k.w["sin2s"][:, :], inc="IS") \
        if "sin2s" in k.w else None
    k.tok_perm = p.dma("gpsimd", k.perm[:], k.w["perm"][:, :], inc="IP") \
        if "perm" in k.w else None
    k.tok_masks = p.dma("gpsimd", coldst(k.masks[:], 512), colsrc(k.w["masks"], 0, 512),
                        inc="IM") if "masks" in k.w else None
    k.tok_gw = p.dma("gpsimd", k.gw[:], k.w["gw6"][:, :], inc="IG") \
        if "gw6" in k.w else None
    k.tok_consts = p.dma("sync", k.consts[:], k.w["consts"][:, :], inc="LD")
    k.tok_ones = p.emit("vector", "memset", k.ones[:], 1.0, inc="VE")
    k.war_xt = [None, None]
    k.war_xlt = [None, None]
    k.war_ob = [None, None]
    k.war_sg = [None, None]
    k.war_vst = [None, None, None, None]
    k.war_eb = [None, None, None, None]
    k.war_qtb = [None, None]
    k.war_rtmp = [None, None]
    k.war_S_factor = [None] * 8
    k.side_step = None
    k.preln = set()
    k.ln_store_tok = {}
    k.ln_hook = None
    k.pre_a3 = None
    k.xlnT_war = None
    k.fl_tok = None
    k.fsum_done = None
    k.gc_war = [None, None]


def arn(k, col, width):
    return k.ARN[:, col:col + width]




def ln_tile(k, xsrc, tt, xcopy=None, extra=()):
    p = k.p
    b = tt % 2
    xt = k.S[:, b * 2048:(b + 1) * 2048]
    ld = p.dma("sync", xt, xsrc[tt * 128:(tt + 1) * 128, :],
               waits=list(extra) + [k.war_xt[b]], inc=f"XT{b}")
    bn = None
    for i in range(4):
        bn = p.emit("vector", "bn_stats", k.stats[:, i * 6:(i + 1) * 6],
                    xt[:, i * 512:(i + 1) * 512], waits=[ld], inc="VE")
    ag = p.emit("vector", "bn_aggr", k.mv[:], k.stats[:], waits=[bn], inc="VE")
    sq = p.emit("scalar", "activation", k.rstd[:, 0:1], k.mv[:, 1:2], AF.Sqrt,
                waits=[ag, k.tok_consts], inc="SE", bias=EPS)
    rc = p.emit("vector", "reciprocal", k.rstd[:, 1:2], k.rstd[:, 0:1],
                waits=[sq], inc="VE")
    xl = k.xlt[:, b * D:(b + 1) * D]
    ts = p.emit("vector", "tensor_scalar", xl, xt, k.mv[:, 0:1], k.rstd[:, 1:2],
                OP.subtract, waits=[rc, k.war_xlt[b]], inc="VE", op1=OP.mult)
    st1 = p.dma("sync", k.xln_s[tt * 128:(tt + 1) * 128, :], xl,
                waits=[ts, p.full("TR")], inc=f"XL{b}")
    k.ln_store_tok[tt] = st1
    k.war_xlt[b] = st1
    k.war_xt[b] = ts
    if xcopy is not None:
        st2 = p.dma("sync", xcopy[tt * 128:(tt + 1) * 128, :], xt,
                    waits=[ts], inc=f"XC{b}")
        k.war_xt[b] = st2
        return st2
    return None


def ln_pass(k, xsrc, xcopy=None, entry_tiles=None, entry=()):
    p = k.p
    tile_copy = [None] * NT
    for tt in range(NT):
        if tt in k.preln:
            continue
        waits = list(entry) + (entry_tiles[tt] if entry_tiles is not None else [])
        tile_copy[tt] = ln_tile(k, xsrc, tt, xcopy=xcopy, extra=waits)
    k.preln = set()
    xw = k.xlnT_war if k.xlnT_war is not None else p.full("MM")
    tr_toks = []
    for g in range(4):
        stok = [k.ln_store_tok[tt] for tt in range(4 * g, 4 * g + 4)]
        t = None
        for dch in range(ND):
            t = p.dmat(arn(k, XLNT0 + dch * T + g * 512, 512),
                       k.xln_s[g * 512:(g + 1) * 512, dch * 128:(dch + 1) * 128],
                       waits=stok + [xw])
        tr_toks.append(t)
    return tr_toks, tile_copy



class FsumEmitter:
    def __init__(self, k):
        self.k = k
        self.units = [(dch, qr) for dch in range(ND) for qr in range(4)]
        self.idx = 0
        self.war_acc = None

    def unit(self, dch, qr):
        k, p = self.k, self.k.p
        toks = []
        for r in range(NE):
            war = k.war_S_factor[r]
            if war is None:
                war = k.war_xt[0 if r < 4 else 1]
            toks.append(p.dma("sync", k.S[:, r * 512:(r + 1) * 512],
                              k.w["factors"][r, dch * 128:(dch + 1) * 128,
                                             qr * 512:(qr + 1) * 512],
                              waits=[war], inc=f"FS{r}"))
        a = p.emit("gpsimd", "tensor_add", k.acc[:], k.S[:, 0:512], k.S[:, 512:1024],
                   waits=[toks[0], toks[1], self.war_acc], inc="GE")
        k.war_S_factor[0] = a
        k.war_S_factor[1] = a
        for r in range(2, NE):
            a = p.emit("gpsimd", "tensor_add", k.acc[:], k.acc[:],
                       k.S[:, r * 512:(r + 1) * 512], waits=[toks[r], a], inc="GE")
            k.war_S_factor[r] = a
        b = (dch * 4 + qr) % 2
        dst = k.xlt[:, b * D:b * D + 512]
        cp = p.emit("gpsimd", "tensor_copy", dst, k.acc[:],
                    waits=[a, k.war_xlt[b]], inc="GE")
        self.war_acc = cp
        st = p.dma("sync", k.fsum_s[dch * 128:(dch + 1) * 128, qr * 512:(qr + 1) * 512],
                   dst, waits=[cp], inc=f"FSS{b}")
        k.war_xlt[b] = st

    def step(self, n=4):
        for _ in range(n):
            if self.idx < len(self.units):
                self.unit(*self.units[self.idx])
                self.idx += 1

    def finish(self):
        while self.idx < len(self.units):
            self.unit(*self.units[self.idx])
            self.idx += 1
        k, p = self.k, self.k.p
        k.fsum_done = [p.full("FSS0"), p.full("FSS1")]
        k.war_xt[0] = self.war_acc
        k.war_xt[1] = self.war_acc



def attn_phase(k, layer, xsrc, xdst, copy_to_dst, entry_tiles=None,
               prefetch_fsum=False):
    p = k.p
    wq = k.w["l0_wq6"] if layer == 0 else k.w["l1_wq6"]
    wk = k.w["l0_wk6"] if layer == 0 else k.w["l1_wk6"]
    wv = k.w["l0_wv6"] if layer == 0 else k.w["l1_wv6"]
    wo = k.w["l0_woT"] if layer == 0 else k.w["l1_woT"]
    rope = (layer == 1)
    BAND0 = 49152 if layer == 0 else 0

    tr_toks, tile_copy = ln_pass(k, xsrc, xcopy=(xdst if copy_to_dst else None),
                                 entry_tiles=entry_tiles)
    dst_ready = [p.full("XC0"), p.full("XC1")] if copy_to_dst else []

    BLK0, VB0 = 32768, 45056
    blk = lambda slot, b: arn(k, BLK0 + (2 * slot + b) * 2048, 2048)
    blk_tok = {}
    blk_war = {}

    def load_blk(wmat, head, slot, b, first=False):
        blk_tok[(slot, b)] = p.dma(
            "gpsimd", blk(slot, b), wmat[head],
            waits=[blk_war.get((slot, b)), p.full("MM") if first else None],
            inc=f"WB{slot}{b}")

    entry_mm = p.full("MM")
    prev_scores = entry_mm
    acc_tile = [None] * NT
    vready = {}

    load_blk(wv, 0, 2, 0, first=True)
    load_blk(wq, 0, 0, 0, first=True)
    load_blk(wk, 0, 1, 0, first=True)
    load_blk(wv, 1, 2, 1)

    def emit_vproj(i):
        b = i % 2
        wtok = blk_tok[(2, b)]
        bank = 4
        evacs = []
        last_mm = None
        for g in range(4):
            mm = None
            for tt4 in range(4):
                tt = g * 4 + tt4
                for dch in range(ND):
                    w0 = []
                    if dch == 0:
                        w0 = [wtok, tr_toks[g]]
                        if tt4 == 0:
                            w0 = w0 + k.pwar[bank]
                    mm = p.emit("tensor", "matmul",
                                k.P[bank][:, tt4 * 128:(tt4 + 1) * 128],
                                arn(k, XLNT0 + dch * T + tt * 128, 128),
                                blk(2, b)[:, dch * 128:(dch + 1) * 128],
                                waits=w0,
                                inc="MM" if (dch == ND - 1 and tt4 == 3) else None,
                                start=(dch == 0), stop=(dch == ND - 1))
            last_mm = mm
            ev = p.emit("scalar", "activation",
                        arn(k, VB0 + b * 2048 + g * 512, 512),
                        k.P[bank][:], AF.Copy, waits=[mm, prev_scores], inc="SE")
            k.pwar[bank] = [ev]
            evacs.append(ev)
        blk_war[(2, b)] = last_mm
        vready[i] = evacs

    emit_vproj(0)

    for i in range(NH):
        b = i % 2
        if k.side_step is not None:
            k.side_step()
        if i + 1 < NH:
            emit_vproj(i + 1)
            load_blk(wq, i + 1, 0, (i + 1) % 2)
            load_blk(wk, i + 1, 1, (i + 1) % 2)
        if i + 2 < NH:
            load_blk(wv, i + 2, 2, (i + 2) % 2)
        qk_ready = []
        for slot, name in ((0, "q"), (1, "k")):
            wtok = blk_tok[(slot, b)]
            base = 0 if name == "q" else 2
            last_mm = None
            for tc in range(4):
                bank = base + tc % 2
                mm = None
                for dch in range(ND):
                    mm = p.emit("tensor", "matmul", k.P[bank][:],
                                blk(slot, b)[:, dch * 128:(dch + 1) * 128],
                                arn(k, XLNT0 + dch * T + tc * 512, 512),
                                waits=([wtok, tr_toks[tc]] + k.pwar[bank])
                                if dch == 0 else [],
                                inc="MM" if dch == ND - 1 else None,
                                start=(dch == 0), stop=(dch == ND - 1))
                last_mm = mm
                if not rope:
                    dst = (k.qt if name == "q" else k.kt)[:, tc * 512:(tc + 1) * 512]
                    ev = p.emit("scalar", "activation", dst, k.P[bank][:], AF.Copy,
                                waits=[mm, prev_scores], inc="SE")
                    k.pwar[bank] = [ev]
                    qk_ready.append(ev)
                else:
                    rb = 6 if name == "q" else 5
                    qs = 0 if name == "q" else 1
                    cv = p.emit("scalar", "activation", k.qtb[:, qs * 512:(qs + 1) * 512],
                                k.P[bank][:], AF.Copy, waits=[mm, k.war_qtb[qs]],
                                inc="SE")
                    m1 = p.emit("vector", "tensor_mul", k.rtmp[:, 0:512], k.P[bank][:],
                                k.cos2[:, tc * 512:(tc + 1) * 512],
                                waits=[mm, cv, k.tok_cos, k.war_rtmp[0]], inc="VE")
                    k.pwar[bank] = [m1]
                    rmm = p.emit("tensor", "matmul", k.P[rb][:], k.perm[:],
                                 k.qtb[:, qs * 512:(qs + 1) * 512],
                                 waits=[cv, k.tok_perm] + k.pwar[rb],
                                 inc="MM", start=True, stop=True)
                    k.war_qtb[qs] = rmm
                    m2 = p.emit("vector", "tensor_mul", k.rtmp[:, 512:1024], k.P[rb][:],
                                k.sin2s[:, tc * 512:(tc + 1) * 512],
                                waits=[rmm, k.tok_sin, k.war_rtmp[1]], inc="VE")
                    k.pwar[rb] = [m2]
                    dst = (k.qt if name == "q" else k.kt)[:, tc * 512:(tc + 1) * 512]
                    out = p.emit("vector", "tensor_add", dst,
                                 k.rtmp[:, 0:512], k.rtmp[:, 512:1024],
                                 waits=[m1, m2, prev_scores], inc="VE")
                    k.war_rtmp = [out, out]
                    qk_ready.append(out)
            blk_war[(slot, b)] = last_mm
        if layer == 0 and i == NH - 1:
            k.xlnT_war = p.full("MM")
        for qc in range(4):
            pis = list(range(4 * qc, min(4 * qc + 4, NT))) + list(range(0, 4 * qc))
            zb = 6 if qc % 2 == 0 else 5
            db = 7
            for idx, pi in enumerate(pis):
                sbk = idx % 4
                smm = p.emit("tensor", "matmul", k.P[sbk][:],
                             k.kt[:, pi * 128:(pi + 1) * 128],
                             k.qt[:, qc * 512:(qc + 1) * 512],
                             waits=qk_ready + k.pwar[sbk], inc="MM",
                             start=True, stop=True)
                es = idx % 4
                ebuf = k.eb[:, es * 512:(es + 1) * 512]
                ex = p.emit("scalar", "activation", ebuf, k.P[sbk][:], AF.Exp,
                            waits=[smm, k.war_eb[es], k.tok_consts], inc="SE",
                            scale=SCALE_QK, bias=-SOFT_SHIFT)
                k.pwar[sbk] = [ex]
                kd = pi - 4 * qc
                if kd >= 0:
                    ex = p.emit("vector", "tensor_mul", ebuf, ebuf,
                                k.masks[:, kd * 512:(kd + 1) * 512],
                                waits=[ex, k.tok_masks], inc="VE")
                zmm = p.emit("tensor", "matmul", k.P[zb][:],
                             arn(k, VB0 + b * 2048 + pi * 128, 128), ebuf,
                             waits=[ex, vready[i][pi // 4]] +
                             (k.pwar[zb] if idx == 0 else []),
                             inc=None, start=(idx == 0), stop=(idx == len(pis) - 1))
                dmm = p.emit("tensor", "matmul", k.P[db][:], k.ones[:], ebuf,
                             waits=([k.tok_ones] + k.pwar[db]) if idx == 0 else [],
                             inc="MM", start=(idx == 0), stop=(idx == len(pis) - 1))
                k.war_eb[es] = dmm
            rcp = p.emit("vector", "reciprocal", k.rd[:], k.P[db][:],
                         waits=[p.full("MM")], inc="VE")
            k.pwar[db] = [rcp]
            vs = 2 + qc % 2
            ctx = p.emit("vector", "tensor_mul", k.vst[:, vs * 512:(vs + 1) * 512],
                         k.P[zb][:], k.rd[:], waits=[rcp, k.war_vst[vs]], inc="VE")
            k.pwar[zb] = [ctx]
            cst = p.dma("sync",
                        k.ctx2[qc * 4:(qc + 1) * 4, :, i, :].transpose([1, 0, 2]),
                        coldst(k.vst[:, vs * 512:(vs + 1) * 512], 128),
                        waits=[ctx, p.full("CB0"), p.full("CB1")], inc=f"VS{vs}")
            k.war_vst[vs] = cst
        prev_scores = p.full("MM")

    if k.pre_a3 is not None:
        k.pre_a3()
        k.pre_a3 = None
    k.fl_tok = None
    if prefetch_fsum:
        for dch in range(ND):
            p.dma("sync", arn(k, 32768 + dch * T, T),
                  k.fsum_s[dch * 128:(dch + 1) * 128, :],
                  waits=([p.full("MM")] + k.fsum_done) if dch == 0 else [],
                  inc="FL")
        k.fl_tok = p.full("FL")

    band = lambda bb: arn(k, BAND0 + bb * 8192, 8192)
    band_tok = [None, None]
    band_war = [None, None]

    def load_band(dc, bb):
        tok = None
        for fch in range(ND):
            tok = p.dma("gpsimd", band(bb)[:, fch * 512:(fch + 1) * 512],
                        wo[fch * 128:(fch + 1) * 128, dc * 512:(dc + 1) * 512],
                        waits=[band_war[bb], p.full("MM")] if fch == 0 else [],
                        inc=f"BD{bb}")
        band_tok[bb] = tok

    load_band(0, 0)
    cb_tok = [None, None]
    cb_war = [None, None]
    for dc in range(4):
        bb = dc % 2
        if dc + 1 < 4:
            load_band(dc + 1, (dc + 1) % 2)
        for tt in range(NT):
            cb = tt % 2
            cbuf = k.qt if cb == 0 else k.kt
            t1 = p.dma("scalar", coldst(cbuf[:, :], 128), k.ctx2[tt],
                       waits=[cb_war[cb], p.full("MM") if dc == 0 and tt < 2 else None,
                              p.full("VS2"), p.full("VS3")], inc=f"CB{cb}")
            cb_tok[cb] = t1
            bank = 2 + (tt % 2)
            mm = None
            for fch in range(ND):
                mm = p.emit("tensor", "matmul", k.P[bank][:],
                            cbuf[:, fch * 128:(fch + 1) * 128],
                            band(bb)[:, fch * 512:(fch + 1) * 512],
                            waits=([cb_tok[cb], band_tok[bb]] + k.pwar[bank])
                            if fch == 0 else [],
                            inc="MM" if fch == ND - 1 else None,
                            start=(fch == 0), stop=(fch == ND - 1))
            cb_war[cb] = mm
            if tt == NT - 1:
                band_war[bb] = mm
            ob = tt % 2
            ev = p.emit("vector", "tensor_copy", k.ob[:, ob * 512:(ob + 1) * 512],
                        k.P[bank][:], waits=[mm, k.war_ob[ob]], inc="VE")
            k.pwar[bank] = [ev]
            ac = p.dma("gpsimd", xdst[tt * 128:(tt + 1) * 128, dc * 512:(dc + 1) * 512],
                       k.ob[:, ob * 512:(ob + 1) * 512],
                       waits=[ev] + dst_ready, inc=f"AC{ob}", accum_op=OP.add)
            k.war_ob[ob] = ac
            if dc == 3:
                acc_tile[tt] = [ac]
                if k.ln_hook is not None:
                    k.ln_hook(tt, ac)
    if layer == 1:
        k.xlnT_war = None
    bar = [p.full("AC0"), p.full("AC1"), p.full("XC0"), p.full("XC1")]
    return bar, acc_tile



def moe_phase(k, xsrc, xdst, copy_to_dst, entry_tiles=None):
    p = k.p
    tr_toks, tile_copy = ln_pass(k, xsrc, xcopy=(xdst if copy_to_dst else None),
                                 entry_tiles=entry_tiles)
    dst_ready = [p.full("XC0"), p.full("XC1")] if copy_to_dst else []

    lgb = 6
    for tt in range(NT):
        mm = None
        for dch in range(ND):
            mm = p.emit("tensor", "matmul", k.P[lgb][:, 0:NE],
                        arn(k, XLNT0 + dch * T + tt * 128, 128),
                        k.gw[:, dch * NE:(dch + 1) * NE],
                        waits=([tr_toks[tt // 4], k.tok_gw] + k.pwar[lgb])
                        if dch == 0 else [],
                        inc="MM" if dch == ND - 1 else None,
                        start=(dch == 0), stop=(dch == ND - 1))
        r = k.rt
        mx = p.emit("vector", "reduce_max", r[:, 0:1], k.P[lgb][:, 0:NE],
                    axis=AX.X, negate=True, waits=[mm], inc="VE")
        ex = p.emit("scalar", "activation", r[:, 8:16], k.P[lgb][:, 0:NE], AF.Exp,
                    waits=[mx, k.tok_consts], inc="SE", bias=r[:, 0:1])
        k.pwar[lgb] = [ex]
        sm = p.emit("vector", "reduce_sum", r[:, 1:2], r[:, 8:16], axis=AX.X,
                    waits=[ex], inc="VE")
        rc = p.emit("vector", "reciprocal", r[:, 2:3], r[:, 1:2], waits=[sm], inc="VE")
        pr = p.emit("vector", "tensor_scalar", r[:, 8:16], r[:, 8:16], r[:, 2:3], None,
                    OP.mult, waits=[rc], inc="VE")
        m1 = p.emit("vector", "reduce_max", r[:, 3:4], r[:, 8:16], axis=AX.X,
                    waits=[pr], inc="VE")
        lt = p.emit("vector", "tensor_scalar", r[:, 16:24], r[:, 8:16], r[:, 3:4], None,
                    OP.is_lt, waits=[m1], inc="VE")
        pm = p.emit("vector", "tensor_mul", r[:, 16:24], r[:, 8:16], r[:, 16:24],
                    waits=[lt], inc="VE")
        m2 = p.emit("vector", "reduce_max", r[:, 4:5], r[:, 16:24], axis=AX.X,
                    waits=[pm], inc="VE")
        ge = p.emit("vector", "tensor_scalar", r[:, 16:24], r[:, 8:16], r[:, 4:5], None,
                    OP.is_ge, waits=[m2], inc="VE")
        s12 = p.emit("vector", "tensor_add", r[:, 5:6], r[:, 3:4], r[:, 4:5],
                     waits=[m2], inc="VE")
        s12b = p.emit("vector", "tensor_scalar", r[:, 5:6], r[:, 5:6], 1e-20, None,
                      OP.add, waits=[s12], inc="VE")
        rw = p.emit("vector", "reciprocal", r[:, 6:7], r[:, 5:6], waits=[s12b], inc="VE")
        p.emit("vector", "scalar_tensor_tensor", k.cw[:, tt * NE:(tt + 1) * NE],
               r[:, 8:16], r[:, 6:7], r[:, 16:24], OP.mult, OP.mult,
               waits=[rw, ge], inc="VE")
    cw_tok = p.full("VE")

    WD0, WB0, GC0 = 32768, 55296, 63488
    wd_war = [None]
    wd_cur = [None]
    blk_tok = {}
    blk_war = {}

    def load_wd(e_wdT):
        tok = None
        for ft in range(NFR):
            tok = p.dma("gpsimd", arn(k, WD0 + ft * 2048, 2048),
                        e_wdT[ft * 128:(ft + 1) * 128, :],
                        waits=[wd_war[0], p.full("MM")] if ft == 0 else [], inc="WD")
        wd_cur[0] = tok

    def load_blk(garr, uarr, ft0, ft, b):
        for slot, arrx in ((0, garr), (1, uarr)):
            blk_tok[(slot, b)] = p.dma(
                "gpsimd", arn(k, WB0 + (2 * slot + b) * 2048, 2048),
                arrx[ft0 + ft], waits=[blk_war.get((slot, b))], inc=f"WG{slot}{b}")

    experts = [(k.w["wg6"][e], k.w["wu6"][e], k.w["wdT"][e], 0, e) for e in range(NE)]
    experts.append((k.w["swg6"], k.w["swu6"], k.w["swdT"][0:FR, :], 0, None))
    experts.append((k.w["swg6"], k.w["swu6"], k.w["swdT"][FR:FS, :], NFR, None))

    def emit_gu(e, garr, uarr, ft0, ft0_loaded=False):
        gsd = k.g_s[e % 2]
        if not ft0_loaded:
            load_blk(garr, uarr, ft0, 0, 0)
        for ft in range(NFR):
            b = ft % 2
            if ft + 1 < NFR:
                load_blk(garr, uarr, ft0, ft + 1, (ft + 1) % 2)
            for tc in range(4):
                gb = tc % 2
                ub = 2 + tc % 2
                mmg = None
                for dch in range(ND):
                    mmg = p.emit("tensor", "matmul", k.P[gb][:],
                                 arn(k, WB0 + (0 + b) * 2048 + dch * 128, 128),
                                 arn(k, XLNT0 + dch * T + tc * 512, 512),
                                 waits=([blk_tok[(0, b)], tr_toks[tc]] + k.pwar[gb])
                                 if dch == 0 else [],
                                 inc="MM" if dch == ND - 1 else None,
                                 start=(dch == 0), stop=(dch == ND - 1))
                sgb = tc % 2
                sl = p.emit("scalar", "activation", k.sg[:, sgb * 512:(sgb + 1) * 512],
                            k.P[gb][:], AF.Silu, waits=[mmg, k.war_sg[sgb]], inc="SE")
                k.pwar[gb] = [sl]
                mmu = None
                for dch in range(ND):
                    mmu = p.emit("tensor", "matmul", k.P[ub][:],
                                 arn(k, WB0 + (2 + b) * 2048 + dch * 128, 128),
                                 arn(k, XLNT0 + dch * T + tc * 512, 512),
                                 waits=([blk_tok[(1, b)]] + k.pwar[ub])
                                 if dch == 0 else [],
                                 inc="MM" if dch == ND - 1 else None,
                                 start=(dch == 0), stop=(dch == ND - 1))
                vs = 2 + tc % 2
                gmul = p.emit("vector", "tensor_mul", k.vst[:, vs * 512:(vs + 1) * 512],
                              k.P[ub][:], k.sg[:, sgb * 512:(sgb + 1) * 512],
                              waits=[mmu, sl, k.war_vst[vs]], inc="VE")
                k.pwar[ub] = [gmul]
                k.war_sg[sgb] = gmul
                gst = p.dma("sync",
                            gsd[tc * 4:(tc + 1) * 4, :, ft, :].transpose([1, 0, 2]),
                            coldst(k.vst[:, vs * 512:(vs + 1) * 512], 128),
                            waits=[gmul, p.full("GC0"), p.full("GC1")], inc=f"VS{vs}")
                k.war_vst[vs] = gst
            blk_war[(0, b)] = p.full("MM")
            blk_war[(1, b)] = p.full("MM")
        return [p.full("VS2"), p.full("VS3")]

    prev_acc_barrier = []
    acc_tile = [None] * NT

    def emit_down(e, g_done, wd_tok, re_idx, last):
        gsd = k.g_s[e % 2]
        for tt in range(NT):
            gcb = tt % 2
            gdst = coldst(arn(k, GC0 + gcb * 1536, 1408), 128)
            gl1 = p.dma("sync", gdst[0:64], gsd[tt, 0:64],
                        waits=[k.gc_war[gcb]] + g_done, inc=f"GC{gcb}")
            gl = [gl1, p.dma("scalar", gdst[64:128], gsd[tt, 64:128],
                             waits=[k.gc_war[gcb]] + g_done, inc=f"GC{gcb}")]
            tile_acc = []
            for hc in range(4):
                bank = 4 + hc % 2
                mm = None
                for ft in range(NFR):
                    mm = p.emit("tensor", "matmul", k.P[bank][:],
                                arn(k, GC0 + gcb * 1536 + ft * 128, 128),
                                arn(k, WD0 + ft * 2048 + hc * 512, 512),
                                waits=([gl, wd_tok] + k.pwar[bank]) if ft == 0 else [],
                                inc="MM" if ft == NFR - 1 else None,
                                start=(ft == 0), stop=(ft == NFR - 1))
                ob = hc % 2
                if re_idx is not None:
                    ev = p.emit("scalar", "activation", k.ob[:, ob * 512:(ob + 1) * 512],
                                k.P[bank][:], AF.Copy,
                                waits=[mm, k.war_ob[ob], cw_tok], inc="SE",
                                scale=k.cw[:, tt * NE + re_idx:tt * NE + re_idx + 1])
                else:
                    ev = p.emit("scalar", "activation", k.ob[:, ob * 512:(ob + 1) * 512],
                                k.P[bank][:], AF.Copy,
                                waits=[mm, k.war_ob[ob]], inc="SE")
                k.pwar[bank] = [ev]
                ac = p.dma("gpsimd", xdst[tt * 128:(tt + 1) * 128,
                                          hc * 512:(hc + 1) * 512],
                           k.ob[:, ob * 512:(ob + 1) * 512],
                           waits=[ev] + dst_ready + prev_acc_barrier,
                           inc=f"AC{ob}", accum_op=OP.add)
                k.war_ob[ob] = ac
                tile_acc.append(ac)
            if last:
                acc_tile[tt] = [tile_acc[2], tile_acc[3]]
                if k.ln_hook is not None:
                    k.ln_hook(tt, tile_acc[2], tile_acc[3])
            k.gc_war[gcb] = p.full("MM")
        wd_war[0] = p.full("MM")
        return [p.full("AC0"), p.full("AC1")]

    k.gc_war = [p.full("MM"), p.full("MM")]
    for slot in (0, 1):
        for b in (0, 1):
            blk_war[(slot, b)] = p.full("MM")
    wd_tok_for = {}
    load_wd(experts[0][2])
    wd_tok_for[0] = wd_cur[0]
    pending = None
    for ei, (wg, wu, wd, ft0, re_idx) in enumerate(experts):
        g_done = emit_gu(ei, wg, wu, ft0, ft0_loaded=(ei > 0))
        if ei + 1 < len(experts):
            nx = experts[ei + 1]
            load_blk(nx[0], nx[1], nx[3], 0, 0)
        if pending is not None:
            pe, pg, pr = pending
            prev_acc_barrier = emit_down(pe, pg, wd_tok_for[pe], pr, last=False)
            load_wd(experts[pe + 1][2])
            wd_tok_for[pe + 1] = wd_cur[0]
        pending = (ei, g_done, re_idx)
    k.xlnT_war = p.full("MM")
    pe, pg, pr = pending
    prev_acc_barrier = emit_down(pe, pg, wd_tok_for[pe], pr, last=True)
    bar = prev_acc_barrier + [p.full("XC0"), p.full("XC1")]
    return bar, acc_tile



def l1ffn_phase(k, xsrc, out, entry_tiles=None):
    p = k.p
    if k.fl_tok is None:
        em = FsumEmitter(k)
        em.finish()
        for dch in range(ND):
            p.dma("sync", arn(k, 32768 + dch * T, T),
                  k.fsum_s[dch * 128:(dch + 1) * 128, :],
                  waits=([p.full("MM")] + k.fsum_done) if dch == 0 else [], inc="FL")
        k.fl_tok = p.full("FL")

    tr_toks, _ = ln_pass(k, xsrc, entry_tiles=entry_tiles)

    for tt in range(NT):
        for ec in range(4):
            bank = ec % 2
            mm = None
            for dch in range(ND):
                mm = p.emit("tensor", "matmul", k.P[bank][:],
                            arn(k, XLNT0 + dch * T + tt * 128, 128),
                            arn(k, 32768 + dch * T + ec * 512, 512),
                            waits=([tr_toks[tt // 4], k.fl_tok] + k.pwar[bank])
                            if dch == 0 else [],
                            inc="MM" if dch == ND - 1 else None,
                            start=(dch == 0), stop=(dch == ND - 1))
            xb = ec % 2
            xchunk = k.S[:, xb * 2048:xb * 2048 + 512]
            xc = p.dma("scalar", xchunk,
                       xsrc[tt * 128:(tt + 1) * 128, ec * 512:(ec + 1) * 512],
                       waits=[k.war_xt[xb]], inc=f"XR{xb}")
            ob = ec % 2
            st1 = p.emit("vector", "scalar_tensor_tensor",
                         k.ob[:, ob * 512:(ob + 1) * 512], k.P[bank][:],
                         1.0 / math.sqrt(8.0), xchunk,
                         OP.mult, OP.add, waits=[mm, xc, k.war_ob[ob]], inc="VE")
            k.pwar[bank] = [st1]
            k.war_xt[xb] = st1
            so = p.dma("sync", out[tt * 128:(tt + 1) * 128, ec * 512:(ec + 1) * 512],
                       k.ob[:, ob * 512:(ob + 1) * 512], waits=[st1], inc=f"OS{ob}")
            k.war_ob[ob] = so
            p.dma("gpsimd", out[tt * 128:(tt + 1) * 128, ec * 512:(ec + 1) * 512],
                  k.w["bias_t"][:, ec * 512:(ec + 1) * 512],
                  waits=[so], inc="ACB", accum_op=OP.add)
    return [p.full("OS0"), p.full("OS1"), p.full("ACB")]



def finish(k, final_toks):
    k.p.emit("sync", "nop", waits=final_toks)


def build_fused():
    nc = bass.Bass("TRN2", target_bir_lowering=False, debug=False)
    st = ExitStack()
    k = K()
    k.p = Prog()
    x = nc.dram_tensor("x", [T, D], F32, kind="ExternalInput").ap()
    out = nc.dram_tensor("out", [T, D], F32, kind="ExternalOutput").ap()
    xs = nc.dram_tensor("xs", [T, D], F32, kind="Internal").ap()
    with st:
        setup(k, st, nc, {n for n, _ in WSPECS})
        em = FsumEmitter(k)
        k.side_step = em.step

        def mk_hook(xsrc):
            def h(tt, *toks):
                ln_tile(k, xsrc, tt, extra=list(toks))
                k.preln.add(tt)
            return h

        k.pre_a3 = em.finish
        k.ln_hook = mk_hook(xs)
        t1, tiles1 = attn_phase(k, 0, x, xs, copy_to_dst=True)
        k.side_step = None
        k.ln_hook = mk_hook(xs)
        t2, tiles2 = moe_phase(k, xs, xs, copy_to_dst=False)
        k.ln_hook = mk_hook(xs)
        t3, tiles3 = attn_phase(k, 1, xs, xs, copy_to_dst=False, prefetch_fsum=True)
        k.ln_hook = None
        t4 = l1ffn_phase(k, xs, out)
        finish(k, t4)
        k.p.realize(nc)
    return nc


def build_phase(which):
    nc = bass.Bass("TRN2", target_bir_lowering=False, debug=False)
    st = ExitStack()
    k = K()
    k.p = Prog()
    xin = nc.dram_tensor("xin", [T, D], F32, kind="ExternalInput").ap()
    xout = nc.dram_tensor("xout", [T, D], F32, kind="ExternalOutput").ap()
    names = {
        "l0attn": {"l0_wq6", "l0_wk6", "l0_wv6", "l0_woT", "masks", "consts"},
        "moe": {"gw6", "wg6", "wu6", "wdT", "swg6", "swu6", "swdT", "consts"},
        "l1attn": {"l1_wq6", "l1_wk6", "l1_wv6", "l1_woT", "masks", "consts",
                   "cos2", "sin2s", "perm"},
        "l1ffn": {"factors", "bias_t", "consts"},
    }[which]
    with st:
        setup(k, st, nc, names)
        if which == "l0attn":
            t, _ = attn_phase(k, 0, xin, xout, copy_to_dst=True)
        elif which == "moe":
            t, _ = moe_phase(k, xin, xout, copy_to_dst=True)
        elif which == "l1attn":
            t, _ = attn_phase(k, 1, xin, xout, copy_to_dst=True)
        else:
            t = l1ffn_phase(k, xin, xout)
        finish(k, t)
        k.p.realize(nc)
    return nc


PHASE_INPUTS = {
    "l0attn": ["l0_wq6", "l0_wk6", "l0_wv6", "l0_woT", "masks", "consts"],
    "moe": ["gw6", "wg6", "wu6", "wdT", "swg6", "swu6", "swdT", "consts"],
    "l1attn": ["l1_wq6", "l1_wk6", "l1_wv6", "l1_woT", "masks", "consts",
               "cos2", "sin2s", "perm"],
    "l1ffn": ["factors", "bias_t", "consts"],
}

FUSED = True


def kernel(**inputs):
    g = host_prep(inputs)
    x = np.ascontiguousarray(np.asarray(inputs["L0_attn_x"], dtype=np.float32))
    ncores = 8
    if FUSED:
        nc = build_fused()
        in_maps = []
        for c in range(ncores):
            m = {n: g[n] for n, _ in WSPECS}
            m["x"] = x[c]
            in_maps.append(m)
        res = run_bass_kernel_spmd(nc, in_maps, core_ids=list(range(ncores)))
        return np.stack([r["out"] for r in res.results], axis=0)
    cur = [x[c] for c in range(ncores)]
    for ph in ("l0attn", "moe", "l1attn", "l1ffn"):
        nc = build_phase(ph)
        in_maps = []
        for c in range(ncores):
            m = {n: g[n] for n in PHASE_INPUTS[ph]}
            m["xin"] = cur[c]
            in_maps.append(m)
        res = run_bass_kernel_spmd(nc, in_maps, core_ids=list(range(ncores)))
        cur = [r["xout"] for r in res.results]
    return np.stack(cur, axis=0)


# version 16 (final)
# speedup vs baseline: 1.2073x; 1.2073x over previous
import math
import numpy as np
from contextlib import ExitStack

import concourse.bass as bass
import concourse.mybir as mybir
from concourse.bass_utils import run_bass_kernel_spmd

F32 = mybir.dt.float32
BF16 = mybir.dt.bfloat16
AF = mybir.ActivationFunctionType
OP = mybir.AluOpType
AX = mybir.AxisListType

T = 2048
D = 2048
NH = 16
DH = 128
FR = 1408
NE = 8
FS = 2816
NT = T // 128
ND = D // 128
NFR = FR // 128
NFS = FS // 128
SOFT_SHIFT = 25.0
SCALE_QK = 1.0 / math.sqrt(128.0)
EPS = 1e-5
CONST_VALS = [0.0, 1.0, EPS, -SOFT_SHIFT, SCALE_QK]

ENGINES = ("sync", "tensor", "vector", "scalar", "gpsimd")

ARN_COLS = 66560
XLNT0 = 0


class Prog:

    def __init__(self):
        self.ops = {e: [] for e in ENGINES}
        self.cnt = {}

    def emit(self, eng, method, *args, waits=(), inc=None, n=1, **kwargs):
        tok = None
        if inc is not None:
            self.cnt[inc] = self.cnt.get(inc, 0) + n
            tok = (inc, self.cnt[inc])
        ws = []
        for w in waits:
            if w is None:
                continue
            if isinstance(w, list):
                ws.extend(x for x in w if x is not None)
            else:
                ws.append(w)
        self.ops[eng].append((method, args, kwargs, ws, inc, n))
        return tok

    def dma(self, eng, out, in_, waits=(), inc=None, **kw):
        return self.emit(eng, "dma_start", out, in_, waits=waits, inc=inc, n=16, **kw)

    def dmat(self, out, in_, waits=(), inc="TR", eng="sync"):
        return self.emit(eng, "dma_start_transpose", out, in_, waits=waits, inc=inc, n=16)

    def full(self, sem):
        c = self.cnt.get(sem, 0)
        return (sem, c) if c > 0 else None

    def realize(self, nc):
        semnames = sorted(self.cnt.keys())
        with ExitStack() as st:
            sems = {s: st.enter_context(nc.semaphore(s)) for s in semnames}
            block = st.enter_context(nc.Block())
            for eng_name in ENGINES:
                ops = self.ops[eng_name]
                if not ops:
                    continue

                def make(ops=ops):
                    def body(eng):
                        waited = {}
                        for method, args, kwargs, ws, inc, n in ops:
                            for s, v in ws:
                                if waited.get(s, 0) < v:
                                    eng.wait_ge(sems[s], v)
                                    waited[s] = v
                            ins = getattr(eng, method)(*args, **kwargs)
                            if inc is not None:
                                ins.then_inc(sems[inc], n)
                    return body

                getattr(block, eng_name)(make())


class K:
    pass



def tile6(wT, nblk, blk_w=128):
    R = wT.shape[0]
    nr = R // 128
    a = wT.reshape(nr, 128, nblk, blk_w).transpose(2, 1, 0, 3)
    return np.ascontiguousarray(a.reshape(nblk, 128, nr * blk_w))


def host_prep(inputs):
    g = {}
    c = lambda a: np.ascontiguousarray(np.asarray(a, dtype=np.float32))
    g["l0_wq6"] = tile6(c(np.transpose(inputs["L0_attn_W_Q"], (2, 0, 1)).reshape(D, D)), NH)
    g["l0_wk6"] = tile6(c(np.transpose(inputs["L0_attn_W_K"], (2, 0, 1)).reshape(D, D)), NH)
    g["l0_wv6"] = tile6(c(np.transpose(inputs["L0_attn_W_V"], (2, 0, 1)).reshape(D, D)), NH)
    g["l0_woT"] = c(inputs["L0_attn_W_O"].T)
    g["l1_wq6"] = tile6(c(inputs["L1_attn_Wq"].T), NH)
    g["l1_wk6"] = tile6(c(inputs["L1_attn_Wk"].T), NH)
    g["l1_wv6"] = tile6(c(inputs["L1_attn_Wv"].T), NH)
    g["l1_woT"] = c(inputs["L1_attn_Wo"].T)
    gwT = c(inputs["L0_ffn_gate_weight"].T)
    g["gw6"] = np.ascontiguousarray(
        gwT.reshape(ND, 128, NE).transpose(1, 0, 2).reshape(128, ND * NE))
    wgT = c(np.transpose(inputs["L0_ffn_w_gate"], (0, 2, 1)))
    wuT = c(np.transpose(inputs["L0_ffn_w_up"], (0, 2, 1)))
    g["wg6"] = np.ascontiguousarray(np.stack([tile6(wgT[e], NFR) for e in range(NE)]))
    g["wu6"] = np.ascontiguousarray(np.stack([tile6(wuT[e], NFR) for e in range(NE)]))
    g["wdT"] = c(np.transpose(inputs["L0_ffn_w_down"], (0, 2, 1)))
    g["swg6"] = tile6(c(inputs["L0_ffn_sw_gate"].T), NFS)
    g["swu6"] = tile6(c(inputs["L0_ffn_sw_up"].T), NFS)
    g["swdT"] = c(inputs["L0_ffn_sw_down"].T)
    g["factors"] = c(inputs["L1_ffn_factors"])
    g["bias_t"] = c(np.tile(np.asarray(inputs["L1_ffn_bias"], np.float32)[None, :], (128, 1)))
    inv = 1.0 / (10000.0 ** (np.arange(0, DH, 2, dtype=np.float64) / DH))
    fr = np.outer(inv, np.arange(T, dtype=np.float64))
    cos = np.cos(fr)
    sin = np.sin(fr)
    g["cos2"] = c(np.concatenate([cos, cos], axis=0))
    g["sin2s"] = c(np.concatenate([-sin, sin], axis=0))
    perm = np.zeros((128, 128), dtype=np.float32)
    for h in range(128):
        perm[(h + 64) % 128, h] = 1.0
    g["perm"] = c(perm)
    p = np.arange(128)[:, None]
    q = np.arange(512)[None, :]
    masks = np.stack([(p <= q - 128 * k).astype(np.float32) for k in range(4)])
    g["masks"] = c(masks.reshape(4 * 128, 512))
    g["consts"] = c(np.tile(np.array(CONST_VALS, dtype=np.float64)[None, :], (128, 1)))
    return g


WSPECS = [
    ("l0_wq6", (NH, 128, D)), ("l0_wk6", (NH, 128, D)), ("l0_wv6", (NH, 128, D)),
    ("l0_woT", (D, D)),
    ("l1_wq6", (NH, 128, D)), ("l1_wk6", (NH, 128, D)), ("l1_wv6", (NH, 128, D)),
    ("l1_woT", (D, D)),
    ("gw6", (128, ND * NE)),
    ("wg6", (NE, NFR, 128, D)), ("wu6", (NE, NFR, 128, D)), ("wdT", (NE, FR, D)),
    ("swg6", (NFS, 128, D)), ("swu6", (NFS, 128, D)), ("swdT", (FS, D)),
    ("factors", (NE, D, D)), ("bias_t", (128, D)),
    ("cos2", (128, T)), ("sin2s", (128, T)), ("perm", (128, 128)),
    ("masks", (4 * 128, 512)), ("consts", (128, len(CONST_VALS))),
]



def colsrc(w, col0, width, p=128):
    return w[:, col0:col0 + width].rearrange("(c p) h -> c p h", p=p).transpose([1, 0, 2])


def coldst(sb_ap, width):
    return sb_ap.rearrange("p (c h) -> p c h", h=width)


def setup(k, st, nc, wnames):
    k.nc = nc
    p = k.p
    k.w = {}
    for name, shape in WSPECS:
        if name in wnames:
            k.w[name] = nc.dram_tensor(name, list(shape), F32, kind="ExternalInput").ap()

    k.xln_s = nc.dram_tensor("xln_s", [T, D], BF16, kind="Internal").ap()
    k.ctx2 = nc.dram_tensor("ctx2", [NT, 128, NH, 128], BF16, kind="Internal").ap()
    k.g_s = [nc.dram_tensor("g_s0", [NT, 128, NFR, 128], BF16, kind="Internal").ap(),
             nc.dram_tensor("g_s1", [NT, 128, NFR, 128], BF16, kind="Internal").ap()]
    k.fsum_s = nc.dram_tensor("fsum_s", [D, D], BF16, kind="Internal").ap()

    sb = lambda name, shape, dt: st.enter_context(nc.sbuf_tensor(name, shape, dt))
    k.ARN = sb("ARN", [128, ARN_COLS], BF16)
    k.S = sb("S", [128, 4096], F32)
    k.qt = sb("qt", [128, T], BF16)
    k.kt = sb("kt", [128, T], BF16)
    k.eb = sb("eb", [128, 2048], BF16)
    k.masks = sb("masks_sb", [128, 2048], BF16)
    k.vst = sb("vst", [128, 2048], BF16)
    k.qtb = sb("qtb", [128, 1024], BF16)
    k.xlt = sb("xlt", [128, 2 * D], BF16)
    k.cos2 = sb("cos2_sb", [128, T], BF16)
    k.sin2s = sb("sin2s_sb", [128, T], BF16)
    k.perm = sb("perm_sb", [128, 128], BF16)
    k.gw = sb("gw_sb", [128, ND * NE], BF16)
    k.ones = sb("ones_sb", [128, 128], BF16)
    k.ob = sb("ob", [128, 1024], F32)
    k.sg = sb("sg", [128, 1024], F32)
    k.rtmp = sb("rtmp", [128, 1024], F32)
    k.rd = sb("rd", [128, 512], F32)
    k.acc = sb("acc", [128, 512], F32)
    k.cw = sb("cw_sb", [128, NT * NE], F32)
    k.stats = sb("stats_sb", [128, 24], F32)
    k.mv = sb("mv_sb", [128, 2], F32)
    k.rstd = sb("rstd_sb", [128, 2], F32)
    k.rt = sb("rt_sb", [128, 24], F32)
    k.consts = sb("consts_sb", [128, len(CONST_VALS)], F32)
    for i, cv in enumerate(CONST_VALS):
        nc.const_aps.register(F32, float(cv), k.consts[:, i:i + 1])
        if float(np.float32(cv)) != float(cv):
            nc.const_aps.register(F32, float(np.float32(cv)), k.consts[:, i:i + 1])
    k.P = [st.enter_context(nc.psum_tensor(f"ps{i}", [128, 512], F32)) for i in range(8)]
    k.pwar = [[] for _ in range(8)]
    k.sbuf_left = nc.sbuf_bytes_remaining

    k.tok_cos = p.dma("gpsimd", k.cos2[:], k.w["cos2"][:, :], inc="IC") \
        if "cos2" in k.w else None
    k.tok_sin = p.dma("gpsimd", k.sin2s[:], k.w["sin2s"][:, :], inc="IS") \
        if "sin2s" in k.w else None
    k.tok_perm = p.dma("gpsimd", k.perm[:], k.w["perm"][:, :], inc="IP") \
        if "perm" in k.w else None
    k.tok_masks = p.dma("gpsimd", coldst(k.masks[:], 512), colsrc(k.w["masks"], 0, 512),
                        inc="IM") if "masks" in k.w else None
    k.tok_gw = p.dma("gpsimd", k.gw[:], k.w["gw6"][:, :], inc="IG") \
        if "gw6" in k.w else None
    k.tok_consts = p.dma("sync", k.consts[:], k.w["consts"][:, :], inc="LD")
    k.tok_ones = p.emit("vector", "memset", k.ones[:], 1.0, inc="VE")
    k.war_xt = [None, None]
    k.war_xlt = [None, None]
    k.war_ob = [None, None]
    k.war_sg = [None, None]
    k.war_vst = [None, None, None, None]
    k.war_eb = [None, None, None, None]
    k.war_qtb = [None, None]
    k.war_rtmp = [None, None]
    k.war_S_factor = [None] * 8
    k.side_step = None
    k.preln = set()
    k.ln_store_tok = {}
    k.ln_hook = None
    k.pre_a3 = None
    k.xlnT_war = None
    k.fl_tok = None
    k.fsum_done = None
    k.gc_war = [None, None]


def arn(k, col, width):
    return k.ARN[:, col:col + width]




def ln_tile(k, xsrc, tt, xcopy=None, extra=()):
    p = k.p
    b = tt % 2
    xt = k.S[:, b * 2048:(b + 1) * 2048]
    ld = p.dma("sync", xt, xsrc[tt * 128:(tt + 1) * 128, :],
               waits=list(extra) + [k.war_xt[b]], inc=f"XT{b}")
    bn = None
    for i in range(4):
        bn = p.emit("vector", "bn_stats", k.stats[:, i * 6:(i + 1) * 6],
                    xt[:, i * 512:(i + 1) * 512], waits=[ld], inc="VE")
    ag = p.emit("vector", "bn_aggr", k.mv[:], k.stats[:], waits=[bn], inc="VE")
    sq = p.emit("scalar", "activation", k.rstd[:, 0:1], k.mv[:, 1:2], AF.Sqrt,
                waits=[ag, k.tok_consts], inc="SE", bias=EPS)
    rc = p.emit("vector", "reciprocal", k.rstd[:, 1:2], k.rstd[:, 0:1],
                waits=[sq], inc="VE")
    xl = k.xlt[:, b * D:(b + 1) * D]
    ts = p.emit("vector", "tensor_scalar", xl, xt, k.mv[:, 0:1], k.rstd[:, 1:2],
                OP.subtract, waits=[rc, k.war_xlt[b]], inc="VE", op1=OP.mult)
    st1 = p.dma("sync", k.xln_s[tt * 128:(tt + 1) * 128, :], xl,
                waits=[ts] + [p.full(f"TR{g}") for g in range(4)], inc=f"XL{b}")
    k.ln_store_tok[tt] = st1
    k.war_xlt[b] = st1
    k.war_xt[b] = ts
    if xcopy is not None:
        st2 = p.dma("sync", xcopy[tt * 128:(tt + 1) * 128, :], xt,
                    waits=[ts], inc=f"XC{b}")
        k.war_xt[b] = st2
        return st2
    return None


def ln_pass(k, xsrc, xcopy=None, entry_tiles=None, entry=()):
    p = k.p
    tile_copy = [None] * NT
    for tt in range(NT):
        if tt in k.preln:
            continue
        waits = list(entry) + (entry_tiles[tt] if entry_tiles is not None else [])
        tile_copy[tt] = ln_tile(k, xsrc, tt, xcopy=xcopy, extra=waits)
    k.preln = set()
    xw = k.xlnT_war if k.xlnT_war is not None else p.full("MM")
    tr_toks = []
    for g in range(4):
        stok = [k.ln_store_tok[tt] for tt in range(4 * g, 4 * g + 4)]
        t = None
        for dch in range(ND):
            t = p.dmat(arn(k, XLNT0 + dch * T + g * 512, 512),
                       k.xln_s[g * 512:(g + 1) * 512, dch * 128:(dch + 1) * 128],
                       waits=stok + [xw], inc=f"TR{g}")
        tr_toks.append(t)
    return tr_toks, tile_copy



class FsumEmitter:
    def __init__(self, k):
        self.k = k
        self.units = [(dch, qr) for dch in range(ND) for qr in range(4)]
        self.idx = 0
        self.war_acc = None

    def unit(self, dch, qr):
        k, p = self.k, self.k.p
        toks = []
        for r in range(NE):
            war = k.war_S_factor[r]
            if war is None:
                war = k.war_xt[0 if r < 4 else 1]
            toks.append(p.dma("sync", k.S[:, r * 512:(r + 1) * 512],
                              k.w["factors"][r, dch * 128:(dch + 1) * 128,
                                             qr * 512:(qr + 1) * 512],
                              waits=[war], inc=f"FS{r}"))
        a = p.emit("gpsimd", "tensor_add", k.acc[:], k.S[:, 0:512], k.S[:, 512:1024],
                   waits=[toks[0], toks[1], self.war_acc], inc="GE")
        k.war_S_factor[0] = a
        k.war_S_factor[1] = a
        for r in range(2, NE):
            a = p.emit("gpsimd", "tensor_add", k.acc[:], k.acc[:],
                       k.S[:, r * 512:(r + 1) * 512], waits=[toks[r], a], inc="GE")
            k.war_S_factor[r] = a
        b = (dch * 4 + qr) % 2
        dst = k.xlt[:, b * D:b * D + 512]
        cp = p.emit("gpsimd", "tensor_copy", dst, k.acc[:],
                    waits=[a, k.war_xlt[b]], inc="GE")
        self.war_acc = cp
        st = p.dma("sync", k.fsum_s[dch * 128:(dch + 1) * 128, qr * 512:(qr + 1) * 512],
                   dst, waits=[cp], inc=f"FSS{b}")
        k.war_xlt[b] = st

    def step(self, n=4):
        for _ in range(n):
            if self.idx < len(self.units):
                self.unit(*self.units[self.idx])
                self.idx += 1

    def finish(self):
        while self.idx < len(self.units):
            self.unit(*self.units[self.idx])
            self.idx += 1
        k, p = self.k, self.k.p
        k.fsum_done = [p.full("FSS0"), p.full("FSS1")]
        k.war_xt[0] = self.war_acc
        k.war_xt[1] = self.war_acc



def attn_phase(k, layer, xsrc, xdst, copy_to_dst, entry_tiles=None,
               prefetch_fsum=False):
    p = k.p
    wq = k.w["l0_wq6"] if layer == 0 else k.w["l1_wq6"]
    wk = k.w["l0_wk6"] if layer == 0 else k.w["l1_wk6"]
    wv = k.w["l0_wv6"] if layer == 0 else k.w["l1_wv6"]
    wo = k.w["l0_woT"] if layer == 0 else k.w["l1_woT"]
    rope = (layer == 1)
    BAND0 = 49152 if layer == 0 else 0

    tr_toks, tile_copy = ln_pass(k, xsrc, xcopy=(xdst if copy_to_dst else None),
                                 entry_tiles=entry_tiles)
    dst_ready = [p.full("XC0"), p.full("XC1")] if copy_to_dst else []

    BLK0, VB0 = 32768, 45056
    blk = lambda slot, b: arn(k, BLK0 + (2 * slot + b) * 2048, 2048)
    blk_tok = {}
    blk_war = {}

    def load_blk(wmat, head, slot, b, first=False):
        blk_tok[(slot, b)] = p.dma(
            "gpsimd", blk(slot, b), wmat[head],
            waits=[blk_war.get((slot, b)), p.full("MM") if first else None],
            inc=f"WB{slot}{b}")

    entry_mm = p.full("MM")
    prev_scores = entry_mm
    acc_tile = [None] * NT
    vready = {}

    load_blk(wv, 0, 2, 0, first=True)
    load_blk(wq, 0, 0, 0, first=True)
    load_blk(wk, 0, 1, 0, first=True)
    load_blk(wv, 1, 2, 1)

    def emit_vproj(i):
        b = i % 2
        wtok = blk_tok[(2, b)]
        bank = 4
        evacs = []
        last_mm = None
        for g in range(4):
            mm = None
            for tt4 in range(4):
                tt = g * 4 + tt4
                for dch in range(ND):
                    w0 = []
                    if dch == 0:
                        w0 = [wtok, tr_toks[g]]
                        if tt4 == 0:
                            w0 = w0 + k.pwar[bank]
                    mm = p.emit("tensor", "matmul",
                                k.P[bank][:, tt4 * 128:(tt4 + 1) * 128],
                                arn(k, XLNT0 + dch * T + tt * 128, 128),
                                blk(2, b)[:, dch * 128:(dch + 1) * 128],
                                waits=w0,
                                inc="MM" if (dch == ND - 1 and tt4 == 3) else None,
                                start=(dch == 0), stop=(dch == ND - 1))
            last_mm = mm
            ev = p.emit("scalar", "activation",
                        arn(k, VB0 + b * 2048 + g * 512, 512),
                        k.P[bank][:], AF.Copy, waits=[mm, prev_scores], inc="SE")
            k.pwar[bank] = [ev]
            evacs.append(ev)
        blk_war[(2, b)] = last_mm
        vready[i] = evacs

    emit_vproj(0)

    for i in range(NH):
        b = i % 2
        if k.side_step is not None:
            k.side_step()
        if i + 1 < NH:
            emit_vproj(i + 1)
            load_blk(wq, i + 1, 0, (i + 1) % 2)
            load_blk(wk, i + 1, 1, (i + 1) % 2)
        if i + 2 < NH:
            load_blk(wv, i + 2, 2, (i + 2) % 2)
        qk_ready = []
        for slot, name in ((0, "q"), (1, "k")):
            wtok = blk_tok[(slot, b)]
            base = 0 if name == "q" else 2
            last_mm = None
            for tc in range(4):
                bank = base + tc % 2
                mm = None
                for dch in range(ND):
                    mm = p.emit("tensor", "matmul", k.P[bank][:],
                                blk(slot, b)[:, dch * 128:(dch + 1) * 128],
                                arn(k, XLNT0 + dch * T + tc * 512, 512),
                                waits=([wtok, tr_toks[tc]] + k.pwar[bank])
                                if dch == 0 else [],
                                inc="MM" if dch == ND - 1 else None,
                                start=(dch == 0), stop=(dch == ND - 1))
                last_mm = mm
                if not rope:
                    dst = (k.qt if name == "q" else k.kt)[:, tc * 512:(tc + 1) * 512]
                    ev = p.emit("scalar", "activation", dst, k.P[bank][:], AF.Copy,
                                waits=[mm, prev_scores], inc="SE")
                    k.pwar[bank] = [ev]
                    qk_ready.append(ev)
                else:
                    rb = 6 if name == "q" else 5
                    qs = 0 if name == "q" else 1
                    cv = p.emit("scalar", "activation", k.qtb[:, qs * 512:(qs + 1) * 512],
                                k.P[bank][:], AF.Copy, waits=[mm, k.war_qtb[qs]],
                                inc="SE")
                    m1 = p.emit("vector", "tensor_mul", k.rtmp[:, 0:512], k.P[bank][:],
                                k.cos2[:, tc * 512:(tc + 1) * 512],
                                waits=[mm, cv, k.tok_cos, k.war_rtmp[0]], inc="VE")
                    k.pwar[bank] = [m1]
                    rmm = p.emit("tensor", "matmul", k.P[rb][:], k.perm[:],
                                 k.qtb[:, qs * 512:(qs + 1) * 512],
                                 waits=[cv, k.tok_perm] + k.pwar[rb],
                                 inc="MM", start=True, stop=True)
                    k.war_qtb[qs] = rmm
                    m2 = p.emit("vector", "tensor_mul", k.rtmp[:, 512:1024], k.P[rb][:],
                                k.sin2s[:, tc * 512:(tc + 1) * 512],
                                waits=[rmm, k.tok_sin, k.war_rtmp[1]], inc="VE")
                    k.pwar[rb] = [m2]
                    dst = (k.qt if name == "q" else k.kt)[:, tc * 512:(tc + 1) * 512]
                    out = p.emit("vector", "tensor_add", dst,
                                 k.rtmp[:, 0:512], k.rtmp[:, 512:1024],
                                 waits=[m1, m2, prev_scores], inc="VE")
                    k.war_rtmp = [out, out]
                    qk_ready.append(out)
            blk_war[(slot, b)] = last_mm
        if layer == 0 and i == NH - 1:
            k.xlnT_war = p.full("MM")
        for qc in range(4):
            pis = list(range(4 * qc, min(4 * qc + 4, NT))) + list(range(0, 4 * qc))
            zb = 6 if qc % 2 == 0 else 5
            db = 7
            for idx, pi in enumerate(pis):
                sbk = idx % 2
                smm = p.emit("tensor", "matmul", k.P[sbk][:],
                             k.kt[:, pi * 128:(pi + 1) * 128],
                             k.qt[:, qc * 512:(qc + 1) * 512],
                             waits=qk_ready + k.pwar[sbk], inc="MM",
                             start=True, stop=True)
                es = idx % 4
                ebuf = k.eb[:, es * 512:(es + 1) * 512]
                ex = p.emit("scalar", "activation", ebuf, k.P[sbk][:], AF.Exp,
                            waits=[smm, k.war_eb[es], k.tok_consts], inc="SE",
                            scale=SCALE_QK, bias=-SOFT_SHIFT)
                k.pwar[sbk] = [ex]
                kd = pi - 4 * qc
                if kd >= 0:
                    ex = p.emit("vector", "tensor_mul", ebuf, ebuf,
                                k.masks[:, kd * 512:(kd + 1) * 512],
                                waits=[ex, k.tok_masks], inc="VE")
                zmm = p.emit("tensor", "matmul", k.P[zb][:],
                             arn(k, VB0 + b * 2048 + pi * 128, 128), ebuf,
                             waits=[ex, vready[i][pi // 4]] +
                             (k.pwar[zb] if idx == 0 else []),
                             inc=None, start=(idx == 0), stop=(idx == len(pis) - 1))
                dmm = p.emit("tensor", "matmul", k.P[db][:], k.ones[:], ebuf,
                             waits=([k.tok_ones] + k.pwar[db]) if idx == 0 else [],
                             inc="MM", start=(idx == 0), stop=(idx == len(pis) - 1))
                k.war_eb[es] = dmm
            rcp = p.emit("vector", "reciprocal", k.rd[:], k.P[db][:],
                         waits=[p.full("MM")], inc="VE")
            k.pwar[db] = [rcp]
            vs = 2 + qc % 2
            ctx = p.emit("vector", "tensor_mul", k.vst[:, vs * 512:(vs + 1) * 512],
                         k.P[zb][:], k.rd[:], waits=[rcp, k.war_vst[vs]], inc="VE")
            k.pwar[zb] = [ctx]
            cst = p.dma("sync",
                        k.ctx2[qc * 4:(qc + 1) * 4, :, i, :].transpose([1, 0, 2]),
                        coldst(k.vst[:, vs * 512:(vs + 1) * 512], 128),
                        waits=[ctx, p.full("CB0"), p.full("CB1")], inc=f"VS{vs}")
            k.war_vst[vs] = cst
        prev_scores = p.full("MM")

    if k.pre_a3 is not None:
        k.pre_a3()
        k.pre_a3 = None
    k.fl_tok = None
    if prefetch_fsum:
        for dch in range(ND):
            p.dma("sync", arn(k, 32768 + dch * T, T),
                  k.fsum_s[dch * 128:(dch + 1) * 128, :],
                  waits=([p.full("MM")] + k.fsum_done) if dch == 0 else [],
                  inc="FL")
        k.fl_tok = p.full("FL")

    band = lambda bb: arn(k, BAND0 + bb * 8192, 8192)
    band_tok = [None, None]
    band_war = [None, None]

    def load_band(dc, bb):
        tok = None
        for fch in range(ND):
            tok = p.dma("gpsimd", band(bb)[:, fch * 512:(fch + 1) * 512],
                        wo[fch * 128:(fch + 1) * 128, dc * 512:(dc + 1) * 512],
                        waits=[band_war[bb], p.full("MM")] if fch == 0 else [],
                        inc=f"BD{bb}")
        band_tok[bb] = tok

    load_band(0, 0)
    cb_tok = [None, None]
    cb_war = [None, None]
    for dc in range(4):
        bb = dc % 2
        if dc + 1 < 4:
            load_band(dc + 1, (dc + 1) % 2)
        for tt in range(NT):
            cb = tt % 2
            cbuf = k.qt if cb == 0 else k.kt
            t1 = p.dma("scalar", coldst(cbuf[:, :], 128), k.ctx2[tt],
                       waits=[cb_war[cb], p.full("MM") if dc == 0 and tt < 2 else None,
                              p.full("VS2"), p.full("VS3")], inc=f"CB{cb}")
            cb_tok[cb] = t1
            bank = 2 + (tt % 2)
            mm = None
            for fch in range(ND):
                mm = p.emit("tensor", "matmul", k.P[bank][:],
                            cbuf[:, fch * 128:(fch + 1) * 128],
                            band(bb)[:, fch * 512:(fch + 1) * 512],
                            waits=([cb_tok[cb], band_tok[bb]] + k.pwar[bank])
                            if fch == 0 else [],
                            inc="MM" if fch == ND - 1 else None,
                            start=(fch == 0), stop=(fch == ND - 1))
            cb_war[cb] = mm
            if tt == NT - 1:
                band_war[bb] = mm
            ob = tt % 2
            ev = p.emit("vector", "tensor_copy", k.ob[:, ob * 512:(ob + 1) * 512],
                        k.P[bank][:], waits=[mm, k.war_ob[ob]], inc="VE")
            k.pwar[bank] = [ev]
            ac = p.dma("gpsimd", xdst[tt * 128:(tt + 1) * 128, dc * 512:(dc + 1) * 512],
                       k.ob[:, ob * 512:(ob + 1) * 512],
                       waits=[ev] + dst_ready, inc=f"AC{ob}", accum_op=OP.add)
            k.war_ob[ob] = ac
            if dc == 3:
                acc_tile[tt] = [ac]
                if k.ln_hook is not None:
                    k.ln_hook(tt, ac)
    if layer == 1:
        k.xlnT_war = None
    bar = [p.full("AC0"), p.full("AC1"), p.full("XC0"), p.full("XC1")]
    return bar, acc_tile



def moe_phase(k, xsrc, xdst, copy_to_dst, entry_tiles=None):
    p = k.p
    tr_toks, tile_copy = ln_pass(k, xsrc, xcopy=(xdst if copy_to_dst else None),
                                 entry_tiles=entry_tiles)
    dst_ready = [p.full("XC0"), p.full("XC1")] if copy_to_dst else []

    lgb = 6
    for tt in range(NT):
        mm = None
        for dch in range(ND):
            mm = p.emit("tensor", "matmul", k.P[lgb][:, 0:NE],
                        arn(k, XLNT0 + dch * T + tt * 128, 128),
                        k.gw[:, dch * NE:(dch + 1) * NE],
                        waits=([tr_toks[tt // 4], k.tok_gw] + k.pwar[lgb])
                        if dch == 0 else [],
                        inc="MM" if dch == ND - 1 else None,
                        start=(dch == 0), stop=(dch == ND - 1))
        r = k.rt
        mx = p.emit("vector", "reduce_max", r[:, 0:1], k.P[lgb][:, 0:NE],
                    axis=AX.X, negate=True, waits=[mm], inc="VE")
        ex = p.emit("scalar", "activation", r[:, 8:16], k.P[lgb][:, 0:NE], AF.Exp,
                    waits=[mx, k.tok_consts], inc="SE", bias=r[:, 0:1])
        k.pwar[lgb] = [ex]
        sm = p.emit("vector", "reduce_sum", r[:, 1:2], r[:, 8:16], axis=AX.X,
                    waits=[ex], inc="VE")
        rc = p.emit("vector", "reciprocal", r[:, 2:3], r[:, 1:2], waits=[sm], inc="VE")
        pr = p.emit("vector", "tensor_scalar", r[:, 8:16], r[:, 8:16], r[:, 2:3], None,
                    OP.mult, waits=[rc], inc="VE")
        m1 = p.emit("vector", "reduce_max", r[:, 3:4], r[:, 8:16], axis=AX.X,
                    waits=[pr], inc="VE")
        lt = p.emit("vector", "tensor_scalar", r[:, 16:24], r[:, 8:16], r[:, 3:4], None,
                    OP.is_lt, waits=[m1], inc="VE")
        pm = p.emit("vector", "tensor_mul", r[:, 16:24], r[:, 8:16], r[:, 16:24],
                    waits=[lt], inc="VE")
        m2 = p.emit("vector", "reduce_max", r[:, 4:5], r[:, 16:24], axis=AX.X,
                    waits=[pm], inc="VE")
        ge = p.emit("vector", "tensor_scalar", r[:, 16:24], r[:, 8:16], r[:, 4:5], None,
                    OP.is_ge, waits=[m2], inc="VE")
        s12 = p.emit("vector", "tensor_add", r[:, 5:6], r[:, 3:4], r[:, 4:5],
                     waits=[m2], inc="VE")
        s12b = p.emit("vector", "tensor_scalar", r[:, 5:6], r[:, 5:6], 1e-20, None,
                      OP.add, waits=[s12], inc="VE")
        rw = p.emit("vector", "reciprocal", r[:, 6:7], r[:, 5:6], waits=[s12b], inc="VE")
        p.emit("vector", "scalar_tensor_tensor", k.cw[:, tt * NE:(tt + 1) * NE],
               r[:, 8:16], r[:, 6:7], r[:, 16:24], OP.mult, OP.mult,
               waits=[rw, ge], inc="VE")
    cw_tok = p.full("VE")

    WD0, WB0, GC0 = 32768, 55296, 63488
    wd_war = [None]
    wd_cur = [None]
    blk_tok = {}
    blk_war = {}

    def load_wd(e_wdT):
        tok = None
        for ft in range(NFR):
            tok = p.dma("gpsimd", arn(k, WD0 + ft * 2048, 2048),
                        e_wdT[ft * 128:(ft + 1) * 128, :],
                        waits=[wd_war[0], p.full("MM")] if ft == 0 else [], inc="WD")
        wd_cur[0] = tok

    def load_blk(garr, uarr, ft0, ft, b):
        for slot, arrx in ((0, garr), (1, uarr)):
            blk_tok[(slot, b)] = p.dma(
                "gpsimd", arn(k, WB0 + (2 * slot + b) * 2048, 2048),
                arrx[ft0 + ft], waits=[blk_war.get((slot, b))], inc=f"WG{slot}{b}")

    experts = [(k.w["wg6"][e], k.w["wu6"][e], k.w["wdT"][e], 0, e) for e in range(NE)]
    experts.append((k.w["swg6"], k.w["swu6"], k.w["swdT"][0:FR, :], 0, None))
    experts.append((k.w["swg6"], k.w["swu6"], k.w["swdT"][FR:FS, :], NFR, None))

    def emit_gu(e, garr, uarr, ft0, ft0_loaded=False):
        gsd = k.g_s[e % 2]
        if not ft0_loaded:
            load_blk(garr, uarr, ft0, 0, 0)
        for ft in range(NFR):
            b = ft % 2
            if ft + 1 < NFR:
                load_blk(garr, uarr, ft0, ft + 1, (ft + 1) % 2)
            for tc in range(4):
                gb = tc % 2
                ub = 2 + tc % 2
                mmg = None
                for dch in range(ND):
                    mmg = p.emit("tensor", "matmul", k.P[gb][:],
                                 arn(k, WB0 + (0 + b) * 2048 + dch * 128, 128),
                                 arn(k, XLNT0 + dch * T + tc * 512, 512),
                                 waits=([blk_tok[(0, b)], tr_toks[tc]] + k.pwar[gb])
                                 if dch == 0 else [],
                                 inc="MM" if dch == ND - 1 else None,
                                 start=(dch == 0), stop=(dch == ND - 1))
                sgb = tc % 2
                sl = p.emit("scalar", "activation", k.sg[:, sgb * 512:(sgb + 1) * 512],
                            k.P[gb][:], AF.Silu, waits=[mmg, k.war_sg[sgb]], inc="SE")
                k.pwar[gb] = [sl]
                mmu = None
                for dch in range(ND):
                    mmu = p.emit("tensor", "matmul", k.P[ub][:],
                                 arn(k, WB0 + (2 + b) * 2048 + dch * 128, 128),
                                 arn(k, XLNT0 + dch * T + tc * 512, 512),
                                 waits=([blk_tok[(1, b)]] + k.pwar[ub])
                                 if dch == 0 else [],
                                 inc="MM" if dch == ND - 1 else None,
                                 start=(dch == 0), stop=(dch == ND - 1))
                vs = 2 + tc % 2
                gmul = p.emit("vector", "tensor_mul", k.vst[:, vs * 512:(vs + 1) * 512],
                              k.P[ub][:], k.sg[:, sgb * 512:(sgb + 1) * 512],
                              waits=[mmu, sl, k.war_vst[vs]], inc="VE")
                k.pwar[ub] = [gmul]
                k.war_sg[sgb] = gmul
                gst = p.dma("sync",
                            gsd[tc * 4:(tc + 1) * 4, :, ft, :].transpose([1, 0, 2]),
                            coldst(k.vst[:, vs * 512:(vs + 1) * 512], 128),
                            waits=[gmul, p.full("GC0"), p.full("GC1")], inc=f"VS{vs}")
                k.war_vst[vs] = gst
            blk_war[(0, b)] = p.full("MM")
            blk_war[(1, b)] = p.full("MM")
        return [p.full("VS2"), p.full("VS3")]

    prev_acc_barrier = []
    acc_tile = [None] * NT

    def emit_down(e, g_done, wd_tok, re_idx, last):
        gsd = k.g_s[e % 2]
        for tt in range(NT):
            gcb = tt % 2
            gl = p.dma("sync", coldst(arn(k, GC0 + gcb * 1536, 1408), 128), gsd[tt],
                       waits=[k.gc_war[gcb]] + g_done, inc=f"GC{gcb}")
            tile_acc = []
            for hc in range(4):
                bank = 4 + hc % 2
                mm = None
                for ft in range(NFR):
                    mm = p.emit("tensor", "matmul", k.P[bank][:],
                                arn(k, GC0 + gcb * 1536 + ft * 128, 128),
                                arn(k, WD0 + ft * 2048 + hc * 512, 512),
                                waits=([gl, wd_tok] + k.pwar[bank]) if ft == 0 else [],
                                inc="MM" if ft == NFR - 1 else None,
                                start=(ft == 0), stop=(ft == NFR - 1))
                ob = hc % 2
                if re_idx is not None:
                    ev = p.emit("scalar", "activation", k.ob[:, ob * 512:(ob + 1) * 512],
                                k.P[bank][:], AF.Copy,
                                waits=[mm, k.war_ob[ob], cw_tok], inc="SE",
                                scale=k.cw[:, tt * NE + re_idx:tt * NE + re_idx + 1])
                else:
                    ev = p.emit("scalar", "activation", k.ob[:, ob * 512:(ob + 1) * 512],
                                k.P[bank][:], AF.Copy,
                                waits=[mm, k.war_ob[ob]], inc="SE")
                k.pwar[bank] = [ev]
                ac = p.dma("gpsimd", xdst[tt * 128:(tt + 1) * 128,
                                          hc * 512:(hc + 1) * 512],
                           k.ob[:, ob * 512:(ob + 1) * 512],
                           waits=[ev] + dst_ready + prev_acc_barrier,
                           inc=f"AC{ob}", accum_op=OP.add)
                k.war_ob[ob] = ac
                tile_acc.append(ac)
            if last:
                acc_tile[tt] = [tile_acc[2], tile_acc[3]]
                if k.ln_hook is not None:
                    k.ln_hook(tt, tile_acc[2], tile_acc[3])
            k.gc_war[gcb] = p.full("MM")
        wd_war[0] = p.full("MM")
        return [p.full("AC0"), p.full("AC1")]

    k.gc_war = [p.full("MM"), p.full("MM")]
    for slot in (0, 1):
        for b in (0, 1):
            blk_war[(slot, b)] = p.full("MM")
    wd_tok_for = {}
    load_wd(experts[0][2])
    wd_tok_for[0] = wd_cur[0]
    pending = None
    for ei, (wg, wu, wd, ft0, re_idx) in enumerate(experts):
        g_done = emit_gu(ei, wg, wu, ft0, ft0_loaded=(ei > 0))
        if ei + 1 < len(experts):
            nx = experts[ei + 1]
            load_blk(nx[0], nx[1], nx[3], 0, 0)
        if pending is not None:
            pe, pg, pr = pending
            prev_acc_barrier = emit_down(pe, pg, wd_tok_for[pe], pr, last=False)
            load_wd(experts[pe + 1][2])
            wd_tok_for[pe + 1] = wd_cur[0]
        pending = (ei, g_done, re_idx)
    k.xlnT_war = p.full("MM")
    pe, pg, pr = pending
    prev_acc_barrier = emit_down(pe, pg, wd_tok_for[pe], pr, last=True)
    bar = prev_acc_barrier + [p.full("XC0"), p.full("XC1")]
    return bar, acc_tile



def l1ffn_phase(k, xsrc, out, entry_tiles=None):
    p = k.p
    if k.fl_tok is None:
        em = FsumEmitter(k)
        em.finish()
        for dch in range(ND):
            p.dma("sync", arn(k, 32768 + dch * T, T),
                  k.fsum_s[dch * 128:(dch + 1) * 128, :],
                  waits=([p.full("MM")] + k.fsum_done) if dch == 0 else [], inc="FL")
        k.fl_tok = p.full("FL")

    tr_toks, _ = ln_pass(k, xsrc, entry_tiles=entry_tiles)

    for tt in range(NT):
        for ec in range(4):
            bank = ec % 2
            mm = None
            for dch in range(ND):
                mm = p.emit("tensor", "matmul", k.P[bank][:],
                            arn(k, XLNT0 + dch * T + tt * 128, 128),
                            arn(k, 32768 + dch * T + ec * 512, 512),
                            waits=([tr_toks[tt // 4], k.fl_tok] + k.pwar[bank])
                            if dch == 0 else [],
                            inc="MM" if dch == ND - 1 else None,
                            start=(dch == 0), stop=(dch == ND - 1))
            xb = ec % 2
            xchunk = k.S[:, xb * 2048:xb * 2048 + 512]
            xc = p.dma("scalar", xchunk,
                       xsrc[tt * 128:(tt + 1) * 128, ec * 512:(ec + 1) * 512],
                       waits=[k.war_xt[xb]], inc=f"XR{xb}")
            ob = ec % 2
            st1 = p.emit("vector", "scalar_tensor_tensor",
                         k.ob[:, ob * 512:(ob + 1) * 512], k.P[bank][:],
                         1.0 / math.sqrt(8.0), xchunk,
                         OP.mult, OP.add, waits=[mm, xc, k.war_ob[ob]], inc="VE")
            k.pwar[bank] = [st1]
            k.war_xt[xb] = st1
            so = p.dma("sync", out[tt * 128:(tt + 1) * 128, ec * 512:(ec + 1) * 512],
                       k.ob[:, ob * 512:(ob + 1) * 512], waits=[st1], inc=f"OS{ob}")
            k.war_ob[ob] = so
            p.dma("gpsimd", out[tt * 128:(tt + 1) * 128, ec * 512:(ec + 1) * 512],
                  k.w["bias_t"][:, ec * 512:(ec + 1) * 512],
                  waits=[so], inc="ACB", accum_op=OP.add)
    return [p.full("OS0"), p.full("OS1"), p.full("ACB")]



def finish(k, final_toks):
    k.p.emit("sync", "nop", waits=final_toks)


def build_fused():
    nc = bass.Bass("TRN2", target_bir_lowering=False, debug=False)
    st = ExitStack()
    k = K()
    k.p = Prog()
    x = nc.dram_tensor("x", [T, D], F32, kind="ExternalInput").ap()
    out = nc.dram_tensor("out", [T, D], F32, kind="ExternalOutput").ap()
    xs = nc.dram_tensor("xs", [T, D], F32, kind="Internal").ap()
    with st:
        setup(k, st, nc, {n for n, _ in WSPECS})
        em = FsumEmitter(k)
        k.side_step = em.step

        def mk_hook(xsrc):
            def h(tt, *toks):
                ln_tile(k, xsrc, tt, extra=list(toks))
                k.preln.add(tt)
            return h

        k.pre_a3 = em.finish
        k.ln_hook = mk_hook(xs)
        t1, tiles1 = attn_phase(k, 0, x, xs, copy_to_dst=True)
        k.side_step = None
        k.ln_hook = mk_hook(xs)
        t2, tiles2 = moe_phase(k, xs, xs, copy_to_dst=False)
        k.ln_hook = mk_hook(xs)
        t3, tiles3 = attn_phase(k, 1, xs, xs, copy_to_dst=False, prefetch_fsum=True)
        k.ln_hook = None
        t4 = l1ffn_phase(k, xs, out)
        finish(k, t4)
        k.p.realize(nc)
    return nc


def build_phase(which):
    nc = bass.Bass("TRN2", target_bir_lowering=False, debug=False)
    st = ExitStack()
    k = K()
    k.p = Prog()
    xin = nc.dram_tensor("xin", [T, D], F32, kind="ExternalInput").ap()
    xout = nc.dram_tensor("xout", [T, D], F32, kind="ExternalOutput").ap()
    names = {
        "l0attn": {"l0_wq6", "l0_wk6", "l0_wv6", "l0_woT", "masks", "consts"},
        "moe": {"gw6", "wg6", "wu6", "wdT", "swg6", "swu6", "swdT", "consts"},
        "l1attn": {"l1_wq6", "l1_wk6", "l1_wv6", "l1_woT", "masks", "consts",
                   "cos2", "sin2s", "perm"},
        "l1ffn": {"factors", "bias_t", "consts"},
    }[which]
    with st:
        setup(k, st, nc, names)
        if which == "l0attn":
            t, _ = attn_phase(k, 0, xin, xout, copy_to_dst=True)
        elif which == "moe":
            t, _ = moe_phase(k, xin, xout, copy_to_dst=True)
        elif which == "l1attn":
            t, _ = attn_phase(k, 1, xin, xout, copy_to_dst=True)
        else:
            t = l1ffn_phase(k, xin, xout)
        finish(k, t)
        k.p.realize(nc)
    return nc


PHASE_INPUTS = {
    "l0attn": ["l0_wq6", "l0_wk6", "l0_wv6", "l0_woT", "masks", "consts"],
    "moe": ["gw6", "wg6", "wu6", "wdT", "swg6", "swu6", "swdT", "consts"],
    "l1attn": ["l1_wq6", "l1_wk6", "l1_wv6", "l1_woT", "masks", "consts",
               "cos2", "sin2s", "perm"],
    "l1ffn": ["factors", "bias_t", "consts"],
}

FUSED = True


def kernel(**inputs):
    g = host_prep(inputs)
    x = np.ascontiguousarray(np.asarray(inputs["L0_attn_x"], dtype=np.float32))
    ncores = 8
    if FUSED:
        nc = build_fused()
        in_maps = []
        for c in range(ncores):
            m = {n: g[n] for n, _ in WSPECS}
            m["x"] = x[c]
            in_maps.append(m)
        res = run_bass_kernel_spmd(nc, in_maps, core_ids=list(range(ncores)))
        return np.stack([r["out"] for r in res.results], axis=0)
    cur = [x[c] for c in range(ncores)]
    for ph in ("l0attn", "moe", "l1attn", "l1ffn"):
        nc = build_phase(ph)
        in_maps = []
        for c in range(ncores):
            m = {n: g[n] for n in PHASE_INPUTS[ph]}
            m["xin"] = cur[c]
            in_maps.append(m)
        res = run_bass_kernel_spmd(nc, in_maps, core_ids=list(range(ncores)))
        cur = [r["xout"] for r in res.results]
    return np.stack(cur, axis=0)
